# version 07
# baseline (speedup 1.0000x reference)
import numpy as np
import ml_dtypes
from contextlib import ExitStack

import concourse.bass as bass
import concourse.mybir as mybir
from concourse.bass_utils import run_bass_kernel_spmd

F32 = mybir.dt.float32
BF16 = mybir.dt.bfloat16
I32 = mybir.dt.int32
AF = mybir.ActivationFunctionType
ALU = mybir.AluOpType
AX = mybir.AxisListType.X

T = 1024
D = 2048
EPS_LN = 1e-5
EPS_RMS = float(np.finfo(np.float32).eps)
RING = 3
SLOT = 4096


def _slab_table(stop="full"):
    if stop in ("p1y", "p0y"):
        stop = "p1" if stop == "p1y" else "p0"
    slabs = []
    for h in range(2):
        for s in range(16):
            slabs.append(("win", h, s))
        if stop != "p1":
            for hoc in range(16):
                slabs.append(("wout", h, hoc))
    if stop in ("full",):
        for c in range(4):
            for sp in range(8):
                slabs.append(("g", c, sp))
                slabs.append(("u", c, sp))
            for sp in range(8):
                slabs.append(("down", c, sp))
    return slabs


def _slab_free_evt(key):
    kind = key[0]
    if kind == "win":
        _, h, si = key
        return ("sP1", h * 32 + 2 * si + 2)
    if kind == "wout":
        _, h, hoc = key
        return ("sP2", 16 * h + hoc + 1)
    if kind == "g":
        _, c, sp = key
        return ("sP3g", ((c * 8 + sp) * 2 + 1) * 2 + 2)
    if kind == "u":
        _, c, sp = key
        return ("sP3u", ((c * 8 + sp) * 2 + 1) * 2 + 2)
    _, c, sp = key
    return ("sD", ((c * 8 + sp) * 2 + 1) * 2 + 2)


class Rot:

    def __init__(self, sems, nbanks, preset=None):
        self.S = sems
        self.last = [None] * nbanks
        if preset:
            for i, ev in preset.items():
                self.last[i] = ev

    def acquire(self, q, b, free_evt):
        ev = self.last[b]
        if ev is not None:
            q.wait_ge(self.S[ev[0]], ev[1])
        self.last[b] = free_evt


def build_program(stop="full"):
    import os
    V_NOTM = os.environ.get("MK_NOTM") == "1"
    V_NOIP = os.environ.get("MK_NOIP") == "1"
    SLABS = _slab_table(stop)
    SLAB_IDX = {k: i for i, k in enumerate(SLABS)}
    if stop in ("p1y", "p0y"):
        STOPY = stop
        stop = "p1" if stop == "p1y" else "p0"
    else:
        STOPY = None
    V_NOSTAT = os.environ.get("MK_NOSTAT") == "1"
    V_STATDBG = os.environ.get("MK_STATDBG") == "1"
    nc = bass.Bass("TRN2", target_bir_lowering=False, debug=False,
                   detect_race_conditions=False)

    xp = nc.dram_tensor("xp", [128, 8, D], F32, kind="ExternalInput").ap()
    w_in_p = nc.dram_tensor("w_in_p", [16, 128, 16, 256], F32, kind="ExternalInput").ap()
    w_out_p = nc.dram_tensor("w_out_p", [16, 128, 32, 128], F32, kind="ExternalInput").ap()
    wg_p = nc.dram_tensor("wg_p", [32, 128, 16, 256], I32, kind="ExternalInput").ap()
    wu_p = nc.dram_tensor("wu_p", [32, 128, 16, 256], I32, kind="ExternalInput").ap()
    wd_p = nc.dram_tensor("wd_p", [4, 8, 128, 16, 2, 128], I32, kind="ExternalInput").ap()
    bc_p = nc.dram_tensor("bc_p", [128, 4, 512], F32, kind="ExternalInput").ap()
    at_p = nc.dram_tensor("at_p", [128, 512], F32, kind="ExternalInput").ap()
    gs_p = nc.dram_tensor("gs_p", [128, 64], F32, kind="ExternalInput").ap()
    us_p = nc.dram_tensor("us_p", [128, 64], F32, kind="ExternalInput").ap()
    ds_p = nc.dram_tensor("ds_p", [128, 16], F32, kind="ExternalInput").ap()
    nw_p = nc.dram_tensor("nw_p", [128, 16], F32, kind="ExternalInput").ap()
    id_bf = nc.dram_tensor("id_bf", [128, 128], BF16, kind="ExternalInput").ap()
    id_f = nc.dram_tensor("id_f", [128, 128], F32, kind="ExternalInput").ap()
    outp = nc.dram_tensor("outp", [128, 16 * T], F32, kind="ExternalOutput").ap()

    ctx = ExitStack()
    e = ctx.enter_context

    accT = e(nc.sbuf_tensor("accT", [128, 16 * T], F32))
    xnT = e(nc.sbuf_tensor("xnT", [128, 16 * T], BF16))
    ringS = e(nc.sbuf_tensor("ringS", [128, RING * SLOT], BF16))
    identb = e(nc.sbuf_tensor("identb", [128, 128], BF16))
    identf = e(nc.sbuf_tensor("identf", [128, 128], F32))
    gs = e(nc.sbuf_tensor("gs", [128, 64], F32))
    us = e(nc.sbuf_tensor("us", [128, 64], F32))
    ds = e(nc.sbuf_tensor("ds", [128, 16], F32))
    nw = e(nc.sbuf_tensor("nw", [128, 16], F32))
    K = e(nc.sbuf_tensor("K", [128, 32], F32))
    c8 = e(nc.sbuf_tensor("c8", [128, 4], F32))
    ones_c = e(nc.sbuf_tensor("ones_c", [128, 1], BF16))
    neg128 = e(nc.sbuf_tensor("neg128", [1, 512], BF16))
    ones_r = e(nc.sbuf_tensor("ones_r", [1, 128], BF16))
    ones_sq = e(nc.sbuf_tensor("ones_sq", [128, 128], BF16))
    cxn = e(nc.sbuf_tensor("cxn", [1, 1024], BF16))
    csh = e(nc.sbuf_tensor("csh", [1, 1024], F32))
    cshi = e(nc.sbuf_tensor("cshi", [1, 1024], BF16))
    cslo = e(nc.sbuf_tensor("cslo", [1, 1024], BF16))
    ssrow = e(nc.sbuf_tensor("ssrow", [1, 512], F32))
    lnr = e(nc.sbuf_tensor("lnr", [1, 512], F32))
    rstd_r = e(nc.sbuf_tensor("rstd_r", [1, 512], F32))
    rstd_bfr = e(nc.sbuf_tensor("rstd_bfr", [1, 512], BF16))
    uscr = e(nc.sbuf_tensor("uscr", [128, 1024], F32))
    stat = e(nc.sbuf_tensor("stat", [128, 10], F32))
    silu = uscr
    tmpA = uscr[:, 0:512]

    ps03 = e(nc.psum_tensor("ps03", [128, 2048], F32))
    ps45 = e(nc.psum_tensor("ps45", [128, 1024], F32))
    ps6 = e(nc.psum_tensor("ps6", [128, 512], F32))
    ps7 = e(nc.psum_tensor("ps7", [128, 512], F32))

    def bank(i):
        return ps03[:, 512 * i:512 * (i + 1)]

    def dbank(i):
        return (ps45[:, 0:512], ps45[:, 512:1024], ps6[:, 0:512])[i]

    sem_names = ["sX0", "sX1", "sC", "sW0", "sW1", "sW2", "sFS", "sTP", "sTPE", "sLN", "sAXN",
                 "sSQ0", "sVE0", "sVS", "sB7P", "sB7E", "sP1", "sE1", "sP2",
                 "sE2", "sSQH", "sSSmm", "sVE2", "sAR2", "sRST", "sP2X",
                 "sAPP", "sP3g", "sP3u", "sSILU", "sH", "sCSHL", "sD",
                 "sE3D", "sSt", "sCON", "sDV", "sB6E", "sAT"]
    S = {n: e(nc.semaphore(n)) for n in sem_names}

    S0 = 224 * 1024 - 32 - 56 * 1024
    assert nc.sbuf_base <= S0, (nc.sbuf_base, S0)
    at = nc.alloc_sbuf_tensor_at
    xsl = at("xsl", [128, 2 * D], F32, offset=S0 + 0).ap()
    xnb = at("xnb", [128, 2 * D], BF16, offset=S0 + 16384).ap()
    scr0 = at("scr0", [128, D], F32, offset=S0 + 24576).ap()
    bcs = at("bcs", [128, 2048], F32, offset=S0 + 32768).ap()
    scrK = at("scrK", [128, 1024], F32, offset=S0 + 40960).ap()
    ats = at("ats", [128, 512], F32, offset=S0 + 45056).ap()
    cst = at("cst", [128, 1024], F32, offset=S0 + 47104).ap()
    yT = at("yT", [128, 32 * 512], BF16, offset=S0 + 0).ap()
    ahT = at("ahT", [128, 16 * 512], BF16, offset=S0 + 32768).ap()
    sq2 = at("sq2", [128, 2 * 512], BF16, offset=S0 + 49152).ap()
    hT = at("hT", [128, 16 * T], BF16, offset=S0 + 0).ap()

    def slab_slot(idx):
        k = idx % RING
        return ringS[:, SLOT * k:SLOT * (k + 1)]

    def slab_dram(key):
        kind = key[0]
        if kind == "win":
            return w_in_p[key[2]]
        if kind == "wout":
            return w_out_p[key[2]]
        if kind == "g":
            return wg_p[key[1] * 8 + key[2]]
        if kind == "u":
            return wu_p[key[1] * 8 + key[2]]
        return wd_p[key[1], key[2]]

    blk = e(nc.Block())

    @blk.sync
    def _(q):
        q.dma_start(out=identb[:], in_=id_bf[:, :]).then_inc(S["sC"], 16)
        q.dma_start(out=identf[:], in_=id_f[:, :]).then_inc(S["sC"], 16)
        q.dma_start(out=bcs[:], in_=bc_p.rearrange("p a b -> p (a b)")).then_inc(S["sC"], 16)
        q.dma_start(out=ats[:], in_=at_p[:, :]).then_inc(S["sC"], 16)
        q.dma_start(out=gs[:], in_=gs_p[:, :]).then_inc(S["sC"], 16)
        q.dma_start(out=us[:], in_=us_p[:, :]).then_inc(S["sC"], 16)
        q.dma_start(out=ds[:], in_=ds_p[:, :]).then_inc(S["sC"], 16)
        q.dma_start(out=nw[:], in_=nw_p[:, :]).then_inc(S["sC"], 16)
        for g in range(8):
            if g >= 2:
                q.wait_ge(S["sTP"], 8 * (g - 1))
            q.dma_start(out=xsl[:, D * (g % 2):D * (g % 2 + 1)],
                        in_=xp[:, g, :]).then_inc(S["sX" + str(g % 2)], 16)
        if STOPY == "p1y":
            q.wait_ge(S["sE1"], 32)
            q.dma_start(out=outp[:, 0:8192],
                        in_=yT[:].bitcast(F32)).then_inc(S["sSt"], 16)
            q.wait_ge(S["sSt"], 16)
            return
        if STOPY == "p0y":
            q.wait_ge(S["sTPE"], 64)
            q.dma_start(out=outp[:, 0:8192],
                        in_=xnT[:].bitcast(F32)).then_inc(S["sSt"], 16)
            q.wait_ge(S["sSt"], 16)
            return
        if stop == "p0":
            q.wait_ge(S["sTPE"], 64)
        elif stop == "p1":
            q.wait_ge(S["sE1"], 64)
        elif stop == "p2a":
            q.wait_ge(S["sE2"], 32)
        elif stop == "p2b":
            q.wait_ge(S["sB7E"], 4)
        elif stop == "sq":
            q.wait_ge(S["sSQH"], 32)
        elif stop == "ssy":
            q.wait_ge(S["sB7E"], 34)
        elif stop == "p2c":
            q.wait_ge(S["sB6E"], 2)
        elif stop == "attn":
            q.wait_ge(S["sAPP"], 2)
        else:
            q.wait_ge(S["sE3D"], 128)
        q.dma_start(out=outp[:, :], in_=accT[:]).then_inc(S["sSt"], 16)
        q.wait_ge(S["sSt"], 16)

    @blk.gpsimd
    def _(q):
        for n, key in enumerate(SLABS):
            if stop == "p0":
                break
            if stop == "attn" and n >= 64:
                break
            if n >= RING:
                ev = _slab_free_evt(SLABS[n - RING])
                q.wait_ge(S[ev[0]], ev[1])
            q.dma_start(out=slab_slot(n),
                        in_=slab_dram(key)).then_inc(S["sW" + str(n % RING)], 16)

    @blk.tensor
    def _(q):
        rot03 = Rot(S, 4)
        rotD = Rot(S, 3, preset={2: ("sB6E", 2)})
        rot7 = Rot(S, 1)
        b7grp = [0]
        sw_seen = [0, 0, 0]

        def need_slab(key):
            n = SLAB_IDX[key]
            t = 16 * (n // RING + 1)
            if t > sw_seen[n % RING]:
                q.wait_ge(S["sW" + str(n % RING)], t)
                sw_seen[n % RING] = t

        q.wait_ge(S["sC"], 128)

        for g in range(8):
            xs = xsl[:, D * (g % 2):D * (g % 2 + 1)]
            xb = xnb[:, D * (g % 2):D * (g % 2 + 1)]
            for b in range(4):
                jdx = 8 * g + b
                rot03.acquire(q, jdx % 4, ("sTPE", jdx + 1))
                if b == 0:
                    q.wait_ge(S["sLN"], g + 1)
                for mi in range(4):
                    hc = 4 * b + mi
                    last = q.matmul(bank(jdx % 4)[:, 128 * mi:128 * (mi + 1)],
                                    xb[:, 128 * hc:128 * (hc + 1)], identb[:],
                                    start=True, stop=True)
                last.then_inc(S["sTP"], 1)
            for b in range(4):
                jdx = 8 * g + 4 + b
                rot03.acquire(q, jdx % 4, ("sTPE", jdx + 1))
                if b == 0:
                    q.wait_ge(S["sAXN"], g + 1)
                axsrc = bcs if V_NOIP else xs
                for mi in range(4):
                    hc = 4 * b + mi
                    if V_NOTM:
                        last = q.matmul(bank(jdx % 4)[:, 128 * mi:128 * (mi + 1)],
                                        axsrc[:, 128 * hc:128 * (hc + 1)], identf[:],
                                        start=True, stop=True)
                    else:
                        last = q.transpose(bank(jdx % 4)[:, 128 * mi:128 * (mi + 1)],
                                           axsrc[:, 128 * hc:128 * (hc + 1)], identf[:])
                last.then_inc(S["sTP"], 1)

        if stop == "p0":
            return
        q.wait_ge(S["sTPE"], 64)
        q.wait_ge(S["sCON"], 1)
        for th in range(2):
            b7grp[0] += 1
            rot7.acquire(q, 0, ("sB7E", b7grp[0]))
            for hc in range(16):
                last = q.matmul(ps7[0:1, 0:512], ones_c[:, 0:1],
                                xnT[:, hc * T + th * 512: hc * T + th * 512 + 512],
                                start=(hc == 0), stop=(hc == 15))
            last.then_inc(S["sB7P"], 1)

        for h in range(2):
            for s in range(16):
                key = ("win", h, s)
                need_slab(key)
                slab = slab_slot(SLAB_IDX[key])
                for jbl in range(2):
                    lcl = s * 2 + jbl
                    n1 = h * 32 + lcl
                    rot03.acquire(q, lcl % 4, ("sE1", n1 + 1))
                    for hc in range(16):
                        last = q.matmul(
                            bank(lcl % 4),
                            slab[:, hc * 256 + jbl * 128: hc * 256 + jbl * 128 + 128],
                            xnT[:, hc * T + h * 512: hc * T + h * 512 + 512],
                            start=(hc == 0), stop=(hc == 15))
                    last.then_inc(S["sP1"], 1)
            for hoc in range(17):
                if stop == "p1":
                    break
                if hoc < 16:
                    key = ("wout", h, hoc)
                    need_slab(key)
                    slab = slab_slot(SLAB_IDX[key])
                    k2 = 16 * h + hoc
                    if hoc == 0:
                        q.wait_ge(S["sE1"], 32 * (h + 1))
                    rot03.acquire(q, hoc % 4, ("sE2", k2 + 1))
                    for jc in range(32):
                        last = q.matmul(
                            bank(hoc % 4),
                            slab[:, jc * 128: jc * 128 + 128],
                            yT[:, jc * 512: jc * 512 + 512],
                            start=(jc == 0), stop=(jc == 31))
                    last.then_inc(S["sP2"], 1)
                if stop in ("p2a", "sq"):
                    continue
                if hoc >= 1:
                    k2s = 16 * h + hoc - 1
                    q.wait_ge(S["sSQH"], k2s + 1)
                    b7grp[0] += 1
                    rot7.acquire(q, 0, ("sB7E", b7grp[0]))
                    last = q.matmul(ps7[:, 0:512], ones_sq[:, 0:128],
                                    sq2[:, 512 * (k2s % 2): 512 * (k2s % 2) + 512],
                                    start=True, stop=True)
                    last.then_inc(S["sB7P"], 1)
            if stop in ("p1", "p2a", "p2b", "sq", "ssy"):
                continue
            q.wait_ge(S["sRST"], h + 1)
            if h == 1:
                q.wait_ge(S["sB6E"], 1)
            q.matmul(ps6[:, 0:512], ones_r[0:1, 0:128],
                     rstd_bfr[0:1, 0:512],
                     start=True, stop=True).then_inc(S["sP2X"], 1)

        if stop in ("attn", "p1", "p2a", "p2b", "p2c", "sq", "ssy"):
            return
        q.wait_ge(S["sB7E"], 2)
        Kp = 0
        dn = 0
        for c in range(4):
            for sp in range(8):
                gkey = ("g", c, sp)
                ukey = ("u", c, sp)
                need_slab(gkey)
                need_slab(ukey)
                gsl = slab_slot(SLAB_IDX[gkey])
                usl = slab_slot(SLAB_IDX[ukey])
                for ibl in range(2):
                    for th in range(2):
                        gb = 2 * (Kp % 2)
                        rot03.acquire(q, gb, ("sSILU", Kp + 1))
                        for hc in range(16):
                            q.matmul(
                                bank(gb),
                                gsl[:, hc * 256 + ibl * 128: hc * 256 + ibl * 128 + 128],
                                xnT[:, hc * T + th * 512: hc * T + th * 512 + 512],
                                start=(hc == 0), stop=False)
                        last = q.matmul(bank(gb), neg128[0:1, 0:128],
                                        cxn[0:1, th * 512: th * 512 + 512],
                                        start=False, stop=True)
                        last.then_inc(S["sP3g"], 1)
                        ub = 2 * (Kp % 2) + 1
                        rot03.acquire(q, ub, ("sH", Kp + 1))
                        for hc in range(16):
                            q.matmul(
                                bank(ub),
                                usl[:, hc * 256 + ibl * 128: hc * 256 + ibl * 128 + 128],
                                xnT[:, hc * T + th * 512: hc * T + th * 512 + 512],
                                start=(hc == 0), stop=False)
                        last = q.matmul(bank(ub), neg128[0:1, 0:128],
                                        cxn[0:1, th * 512: th * 512 + 512],
                                        start=False, stop=True)
                        last.then_inc(S["sP3u"], 1)
                        Kp += 1
            q.wait_ge(S["sH"], 32 * (c + 1))
            for th in range(2):
                b7grp[0] += 1
                rot7.acquire(q, 0, ("sB7E", b7grp[0]))
                for it in range(16):
                    last = q.matmul(ps7[0:1, 0:512], ones_c[:, 0:1],
                                    hT[:, it * T + th * 512: it * T + th * 512 + 512],
                                    start=(it == 0), stop=(it == 15))
                last.then_inc(S["sB7P"], 1)
            if c == 3:
                q.wait_ge(S["sCSHL"], 1)
            for sp in range(8):
                dkey = ("down", c, sp)
                need_slab(dkey)
                dsl = slab_slot(SLAB_IDX[dkey])
                for hob in range(2):
                    for th in range(2):
                        rotD.acquire(q, dn % 3, ("sE3D", dn + 1))
                        for it in range(16):
                            last = q.matmul(
                                dbank(dn % 3),
                                dsl[:, (it * 2 + hob) * 128: (it * 2 + hob) * 128 + 128],
                                hT[:, it * T + th * 512: it * T + th * 512 + 512],
                                start=(it == 0),
                                stop=(it == 15 and c != 3))
                        if c == 3:
                            q.matmul(dbank(dn % 3), neg128[0:1, 0:128],
                                     cshi[0:1, th * 512: th * 512 + 512],
                                     start=False, stop=False)
                            last = q.matmul(dbank(dn % 3), neg128[0:1, 0:128],
                                            cslo[0:1, th * 512: th * 512 + 512],
                                            start=False, stop=True)
                        last.then_inc(S["sD"], 1)
                        dn += 1

    @blk.vector
    def _(q):
        ts = q.tensor_scalar
        stt = q.scalar_tensor_tensor
        dvn = [0]

        q.memset(ones_c[:], 1.0)
        q.memset(neg128[:], -128.0)
        q.memset(ones_r[:], 1.0)
        q.memset(ones_sq[:], 1.0)
        q.memset(c8[:, 0:1], -0.5)
        q.memset(c8[:, 1:2], 1.5707963267948966).then_inc(S["sCON"], 1)
        def fence0(inst):
            dvn[0] += 1
            inst.then_inc(S["sDV"], 1)
            q.wait_ge(S["sDV"], dvn[0])
        q.memset(c8[:, 3:4], 3.0)
        fence0(q.reciprocal(c8[:, 2:3], c8[:, 3:4]))
        fence0(ts(c8[:, 3:4], c8[:, 2:3], 3.0, 2.0, ALU.mult, ALU.subtract))
        fence0(ts(c8[:, 2:3], c8[:, 3:4], c8[:, 2:3], -1.0, ALU.mult, ALU.mult))
        fence0(ts(c8[:, 2:3], c8[:, 2:3], 4.0, 1.0, ALU.mult, ALU.subtract))
        fence0(ts(c8[:, 2:3], c8[:, 2:3], 3.0, 1.0, ALU.mult, ALU.subtract))
        fence0(ts(c8[:, 3:4], c8[:, 2:3], -1.0, None, ALU.mult))
        fence0(q.tensor_max(c8[:, 2:3], c8[:, 2:3], c8[:, 3:4]))
        q.wait_ge(S["sC"], 128)
        q.wait_ge(S["sAT"], 1)
        q.memset(scrK[:, 512:1024], 0.0)
        q.tensor_mul(uscr[:, 0:512], cst[:, 0:512], scrK[:, 512:1024])
        q.tensor_mul(uscr[:, 512:1024], cst[:, 512:1024], scrK[:, 512:1024])
        q.tensor_sub(uscr[:, 0:512], uscr[:, 0:512], uscr[:, 512:1024])
        q.tensor_add(scrK[:, 0:512], uscr[:, 0:512], bcs[:, 0:512])
        q.tensor_add(uscr[:, 512:1024], uscr[:, 512:1024], uscr[:, 0:512])
        q.tensor_add(uscr[:, 512:1024], uscr[:, 512:1024], bcs[:, 512:1024])
        q.tensor_mul(scrK[:, 0:512], scrK[:, 0:512], bcs[:, 1024:1536])
        q.tensor_mul(uscr[:, 512:1024], uscr[:, 512:1024], bcs[:, 1536:2048])
        q.tensor_sub(scrK[:, 0:512], scrK[:, 0:512], uscr[:, 512:1024])
        q.reduce_sum(K[:].rearrange("p (a b) -> p a b", b=1),
                     scrK[:, 0:512].rearrange("p (a b) -> p a b", a=32), AX)

        for g in range(8):
            xs = xsl[:, D * (g % 2):D * (g % 2 + 1)]
            xb = xnb[:, D * (g % 2):D * (g % 2 + 1)]
            if g >= 2:
                q.wait_ge(S["sTP"], 8 * (g - 1))
            q.wait_ge(S["sX" + str(g % 2)], 16 * (g // 2 + 1))
            if V_NOSTAT:
                q.memset(stat[:, 6:7], 1.0)
                q.memset(stat[:, 8:9], 0.0)
                q.memset(stat[:, 9:10], 2.0)
                ts(xb, xs, stat[:, 6:7], stat[:, 8:9],
                   ALU.mult, ALU.subtract).then_inc(S["sLN"], 1)
                ts(bcs if V_NOIP else xs, xs, stat[:, 9:10], stat[:, 8:9],
                   ALU.mult, ALU.subtract).then_inc(S["sAXN"], 1)
                for b in range(8):
                    jdx = 8 * g + b
                    q.wait_ge(S["sTP"], jdx + 1)
                    src = bank(jdx % 4).rearrange("p (a t) -> p a t", a=4)
                    if b < 4:
                        dst = xnT[:].rearrange("p (a t) -> p a t", a=16)[
                            :, 4 * b:4 * b + 4, g * 128:(g + 1) * 128]
                    else:
                        dst = accT[:].rearrange("p (a t) -> p a t", a=16)[
                            :, 4 * (b - 4):4 * (b - 4) + 4, g * 128:(g + 1) * 128]
                    q.tensor_copy(dst, src).then_inc(S["sTPE"], 1)
                continue
            def fence(inst):
                dvn[0] += 1
                inst.then_inc(S["sDV"], 1)
                q.wait_ge(S["sDV"], dvn[0])
            fence(q.reduce_sum(stat[:, 0:1], xs, AX))
            q.wait_ge(S["sSQ0"], g + 1)
            ts(stat[:, 2:3], stat[:, 0:1], 1.0 / D, None, ALU.mult)
            fence(ts(stat[:, 4:5], stat[:, 1:2], 1.0 / D, None, ALU.mult))
            fence(ts(stat[:, 3:4], stat[:, 2:3], stat[:, 2:3], None, ALU.mult))
            fence(ts(stat[:, 5:6], stat[:, 4:5], stat[:, 3:4], EPS_LN,
                     ALU.subtract, ALU.add))
            q.engine_nop().then_inc(S["sVE0"], 1)
            q.wait_ge(S["sVS"], g + 1)
            fence(ts(stat[:, 7:8], stat[:, 6:7], stat[:, 6:7], stat[:, 5:6],
                     ALU.mult, ALU.mult))
            fence(ts(stat[:, 7:8], stat[:, 7:8], -0.5, 1.5, ALU.mult, ALU.add))
            fence(ts(stat[:, 6:7], stat[:, 7:8], stat[:, 6:7], None, ALU.mult))
            ts(stat[:, 8:9], stat[:, 2:3], stat[:, 6:7], None, ALU.mult)
            fence(ts(stat[:, 9:10], stat[:, 6:7], 1.0, None, ALU.add))
            ts(xb, xs, stat[:, 6:7], stat[:, 8:9],
               ALU.mult, ALU.subtract).then_inc(S["sLN"], 1)
            ts(bcs if V_NOIP else xs, xs, stat[:, 9:10], stat[:, 8:9],
               ALU.mult, ALU.subtract).then_inc(S["sAXN"], 1)
            for b in range(8):
                jdx = 8 * g + b
                q.wait_ge(S["sTP"], jdx + 1)
                src = bank(jdx % 4).rearrange("p (a t) -> p a t", a=4)
                if b < 4:
                    dst = xnT[:].rearrange("p (a t) -> p a t", a=16)[
                        :, 4 * b:4 * b + 4, g * 128:(g + 1) * 128]
                else:
                    dst = accT[:].rearrange("p (a t) -> p a t", a=16)[
                        :, 4 * (b - 4):4 * (b - 4) + 4, g * 128:(g + 1) * 128]
                q.tensor_copy(dst, src).then_inc(S["sTPE"], 1)

        if stop == "p0":
            return
        for th in range(2):
            q.wait_ge(S["sB7P"], th + 1)
            q.tensor_copy(cxn[0:1, th * 512: th * 512 + 512],
                          ps7[0:1, 0:512]).then_inc(S["sB7E"], 1)

        for h in range(2):
            for lcl in range(32):
                n1 = h * 32 + lcl
                jc = lcl
                q.wait_ge(S["sP1"], n1 + 1)
                ts(yT[:, jc * 512: jc * 512 + 512], bank(lcl % 4),
                   K[:, jc:jc + 1], None, ALU.mult).then_inc(S["sE1"], 1)
            if stop == "p1":
                continue
            for hoc in range(17):
                if hoc < 16:
                    k2 = 16 * h + hoc
                    q.wait_ge(S["sP2"], k2 + 1)
                    q.tensor_copy(ahT[:, hoc * 512: hoc * 512 + 512],
                                  bank(hoc % 4)).then_inc(S["sE2"], 1)
                if stop in ("p2a", "sq"):
                    continue
                if hoc >= 1:
                    hs = hoc - 1
                    q.wait_ge(S["sB7P"], 2 + h * 16 + hs + 1)
                    if stop == "ssy":
                        q.memset(ssrow[0:1, 0:512], 1.0).then_inc(S["sB7E"], 1)
                    elif hs == 0:
                        q.tensor_copy(ssrow[0:1, 0:512],
                                      ps7[0:1, 0:512]).then_inc(S["sB7E"], 1)
                    else:
                        q.tensor_add(ssrow[0:1, 0:512], ssrow[0:1, 0:512],
                                     ps7[0:1, 0:512]).then_inc(S["sB7E"], 1)
            if stop in ("p2a", "sq", "p2b", "ssy"):
                continue
            sr = ssrow[0:1, 0:512]
            ts(sr, sr, 1.0 / D, c8[0:1, 2:3], ALU.mult, ALU.add).then_inc(S["sVE2"], 1)
            q.wait_ge(S["sAR2"], h + 1)
            rr = rstd_r[0:1, 0:512]
            lr = lnr[0:1, 0:512]
            q.tensor_mul(lr, rr, rr)
            q.tensor_mul(lr, lr, sr)
            ts(lr, lr, -0.5, 1.5, ALU.mult, ALU.add)
            q.tensor_mul(rr, rr, lr)
            q.tensor_copy(rstd_bfr[0:1, 0:512], rr).then_inc(S["sRST"], 1)
            q.wait_ge(S["sP2X"], h + 1)
            q.tensor_copy(uscr[:, 512:1024], ps6[:, 0:512]).then_inc(S["sB6E"], 1)
            if stop == "p2c":
                continue
            last = None
            for hoc in range(16):
                stt(tmpA, ahT[:, hoc * 512: hoc * 512 + 512],
                    nw[:, hoc:hoc + 1], uscr[:, 512:1024], ALU.mult, ALU.mult)
                a = accT[:, hoc * T + h * 512: hoc * T + h * 512 + 512]
                last = q.tensor_add(a, a, tmpA)
            last.then_inc(S["sAPP"], 1)

        if stop in ("attn", "p1", "p2a", "p2b", "p2c", "sq", "ssy"):
            return
        Kp = 0
        dn = 0
        for c in range(4):
            if c >= 1:
                q.wait_ge(S["sD"], 32 * c)
            for sp in range(8):
                for ibl in range(2):
                    for th in range(2):
                        ib = c * 16 + sp * 2 + ibl
                        q.wait_ge(S["sP3u"], Kp + 1)
                        q.wait_ge(S["sSILU"], Kp + 1)
                        stt(hT[:, (sp * 2 + ibl) * T + th * 512:
                               (sp * 2 + ibl) * T + th * 512 + 512],
                            bank(2 * (Kp % 2) + 1), us[:, ib:ib + 1],
                            silu[:, 512 * (Kp % 2): 512 * (Kp % 2) + 512],
                            ALU.mult, ALU.mult).then_inc(S["sH"], 1)
                        Kp += 1
            for th in range(2):
                g7 = 35 + 2 * c + th
                q.wait_ge(S["sB7P"], g7)
                dst = csh[0:1, th * 512: th * 512 + 512]
                if c == 0:
                    q.tensor_copy(dst, ps7[0:1, 0:512]).then_inc(S["sB7E"], 1)
                else:
                    q.tensor_add(dst, dst, ps7[0:1, 0:512]).then_inc(S["sB7E"], 1)
            if c == 3:
                q.tensor_copy(cshi[:], csh[:])
                q.tensor_sub(cslo[:], csh[:], cshi[:]).then_inc(S["sCSHL"], 1)
            for sp in range(8):
                for hob in range(2):
                    for th in range(2):
                        hoa = sp * 2 + hob
                        q.wait_ge(S["sD"], dn + 1)
                        a = accT[:, hoa * T + th * 512: hoa * T + th * 512 + 512]
                        stt(a, dbank(dn % 3), ds[:, hoa:hoa + 1], a,
                            ALU.mult, ALU.add).then_inc(S["sE3D"], 1)
                        dn += 1

    @blk.scalar
    def _(q):
        q.wait_ge(S["sC"], 128)
        q.wait_ge(S["sCON"], 1)
        q.activation(out=cst[:, 0:512], in_=ats[:], func=AF.Sin,
                     bias=c8[:, 1:2])
        q.activation(out=cst[:, 512:1024], in_=ats[:],
                     func=AF.Sin).then_inc(S["sAT"], 1)
        for g in range(8):
            if V_NOSTAT:
                break
            xs = xsl[:, D * (g % 2):D * (g % 2 + 1)]
            q.wait_ge(S["sX" + str(g % 2)], 16 * (g // 2 + 1))
            if g >= 1:
                q.wait_ge(S["sVE0"], g)
            q.activation(out=scr0[:], in_=xs, func=AF.Square,
                         accum_out=stat[:, 1:2]).then_inc(S["sSQ0"], 1)
            q.wait_ge(S["sVE0"], g + 1)
            q.activation(out=stat[:, 7:8], in_=stat[:, 5:6], func=AF.Ln)
            q.activation(out=stat[:, 6:7], in_=stat[:, 7:8], func=AF.Exp,
                         scale=c8[:, 0:1]).then_inc(S["sVS"], 1)
        if stop in ("p0", "p1", "p2a"):
            return
        for h in range(2):
            for hoc in range(16):
                k2 = 16 * h + hoc
                q.wait_ge(S["sE2"], k2 + 1)
                if k2 >= 2 and stop != "sq":
                    q.wait_ge(S["sB7P"], k2 + 1)
                q.activation(out=sq2[:, 512 * (k2 % 2): 512 * (k2 % 2) + 512],
                             in_=ahT[:, hoc * 512: hoc * 512 + 512],
                             func=AF.Square).then_inc(S["sSQH"], 1)
            if stop in ("p2b", "sq", "ssy"):
                continue
            q.wait_ge(S["sVE2"], h + 1)
            sr = ssrow[0:1, 0:512]
            lr = lnr[0:1, 0:512]
            rr = rstd_r[0:1, 0:512]
            q.activation(out=lr, in_=sr, func=AF.Ln)
            q.activation(out=rr, in_=lr, func=AF.Exp,
                         scale=c8[0:1, 0:1]).then_inc(S["sAR2"], 1)
        if stop in ("attn", "p1", "p2a", "p2b", "p2c", "sq", "ssy"):
            return
        Kp = 0
        for c in range(4):
            for sp in range(8):
                for ibl in range(2):
                    for th in range(2):
                        ib = c * 16 + sp * 2 + ibl
                        q.wait_ge(S["sP3g"], Kp + 1)
                        if Kp == 0:
                            q.wait_ge(S["sAPP"], 2)
                        if Kp >= 2:
                            q.wait_ge(S["sH"], Kp - 1)
                        q.activation(out=silu[:, 512 * (Kp % 2): 512 * (Kp % 2) + 512],
                                     in_=bank(2 * (Kp % 2)), func=AF.Silu,
                                     scale=gs[:, ib:ib + 1]).then_inc(S["sSILU"], 1)
                        Kp += 1

    ctx.close()
    return nc


_CACHE = {}


def _get_nc():
    import os
    stop = os.environ.get("MK_STOP", "full")
    if stop not in _CACHE:
        _CACHE[stop] = build_program(stop)
    return _CACHE[stop]


def kernel(L0_attn_x, L0_attn_W_in, L0_attn_W_out, L0_attn_A_theta,
           L0_attn_B_real, L0_attn_B_imag, L0_attn_C_real, L0_attn_C_imag,
           L0_attn_norm_w, L0_ffn_gate_w, L0_ffn_gate_scale, L0_ffn_up_w,
           L0_ffn_up_scale, L0_ffn_down_w, L0_ffn_down_scale):
    bf = ml_dtypes.bfloat16
    f32 = np.float32
    i32 = np.int32

    x = np.ascontiguousarray(np.asarray(L0_attn_x, dtype=f32).reshape(8192, D))
    W_in = np.asarray(L0_attn_W_in, dtype=f32)
    W_out = np.asarray(L0_attn_W_out, dtype=f32)
    gate_w = np.asarray(L0_ffn_gate_w, dtype=i32)
    up_w = np.asarray(L0_ffn_up_w, dtype=i32)
    down_w = np.asarray(L0_ffn_down_w, dtype=i32)

    w_in_p = np.ascontiguousarray(W_in.reshape(16, 256, 16, 128).transpose(0, 3, 2, 1))
    w_out_p = np.ascontiguousarray(W_out.reshape(16, 128, 32, 128).transpose(0, 3, 2, 1))
    wg_p = np.ascontiguousarray(gate_w.reshape(32, 256, 16, 128).transpose(0, 3, 2, 1))
    wu_p = np.ascontiguousarray(up_w.reshape(32, 256, 16, 128).transpose(0, 3, 2, 1))
    wd_p = np.ascontiguousarray(
        down_w.T.reshape(4, 16, 128, 8, 2, 128).transpose(0, 3, 2, 1, 4, 5))

    def row32(v):
        return np.ascontiguousarray(np.asarray(v, dtype=f32).reshape(32, 128).T)

    bc = np.stack([
        np.asarray(L0_attn_B_real, f32).reshape(4096, 16),
        np.asarray(L0_attn_B_imag, f32).reshape(4096, 16),
        np.asarray(L0_attn_C_real, f32).reshape(4096, 16),
        np.asarray(L0_attn_C_imag, f32).reshape(4096, 16),
    ])
    bc_p = np.ascontiguousarray(
        bc.reshape(4, 32, 128, 16).transpose(2, 0, 1, 3).reshape(128, 4, 512))
    at_p = np.ascontiguousarray(
        np.asarray(L0_attn_A_theta, f32).reshape(4096, 16)
        .reshape(32, 128, 16).transpose(1, 0, 2).reshape(128, 512))
    gs_p = np.ascontiguousarray(np.asarray(L0_ffn_gate_scale, f32).reshape(64, 128).T)
    us_p = np.ascontiguousarray(np.asarray(L0_ffn_up_scale, f32).reshape(64, 128).T)
    ds_p = np.ascontiguousarray(np.asarray(L0_ffn_down_scale, f32).reshape(16, 128).T)
    nw_p = np.ascontiguousarray(np.asarray(L0_attn_norm_w, f32).reshape(16, 128).T)
    id_bf = np.eye(128, dtype=bf)
    id_f = np.eye(128, dtype=f32)

    shared = dict(w_in_p=w_in_p, w_out_p=w_out_p, wg_p=wg_p, wu_p=wu_p,
                  wd_p=wd_p, bc_p=bc_p, at_p=at_p, gs_p=gs_p, us_p=us_p, ds_p=ds_p,
                  nw_p=nw_p, id_bf=id_bf, id_f=id_f)
    in_maps = []
    for c in range(8):
        xs = x[c * T:(c + 1) * T].reshape(8, 128, D).transpose(1, 0, 2)
        m = dict(shared)
        m["xp"] = np.ascontiguousarray(xs)
        in_maps.append(m)

    nc = _get_nc()
    res = run_bass_kernel_spmd(nc, in_maps, core_ids=list(range(8)))

    out = np.empty((8192, D), dtype=f32)
    for c in range(8):
        o = np.asarray(res.results[c]["outp"]).reshape(128, 16, T)
        out[c * T:(c + 1) * T] = o.transpose(2, 1, 0).reshape(T, D)
    return out.reshape(1, 8192, D)


# version 09
# speedup vs baseline: 1.2072x; 1.2072x over previous
import numpy as np
import ml_dtypes
from contextlib import ExitStack

import concourse.bass as bass
import concourse.mybir as mybir
from concourse.bass_utils import run_bass_kernel_spmd

F32 = mybir.dt.float32
BF16 = mybir.dt.bfloat16
I32 = mybir.dt.int32
AF = mybir.ActivationFunctionType
ALU = mybir.AluOpType
AX = mybir.AxisListType.X

T = 1024
D = 2048
EPS_LN = 1e-5
EPS_RMS = float(np.finfo(np.float32).eps)
RING = 5
SLOT = 4096
N_ATTN_SLABS = 64


def _slab_table(stop="full"):
    if stop in ("p1y", "p0y"):
        stop = "p1" if stop == "p1y" else "p0"
    slabs = []
    for h in range(2):
        for s in range(16):
            slabs.append(("win", h, s))
        if stop != "p1":
            for hoc in range(16):
                slabs.append(("wout", h, hoc))
    if stop in ("full",):
        for c in range(4):
            for sp in range(8):
                slabs.append(("g", c, sp))
                slabs.append(("u", c, sp))
            for sp in range(8):
                slabs.append(("down", c, sp))
    return slabs


def _slab_free_evt(key):
    kind = key[0]
    if kind == "win":
        _, h, si = key
        return ("sP1", h * 32 + 2 * si + 2)
    if kind == "wout":
        _, h, hoc = key
        return ("sP2", 16 * h + hoc + 1)
    if kind == "g":
        _, c, sp = key
        return ("sP3g", ((c * 8 + sp) * 2 + 1) * 2 + 2)
    if kind == "u":
        _, c, sp = key
        return ("sP3u", ((c * 8 + sp) * 2 + 1) * 2 + 2)
    _, c, sp = key
    return ("sD", ((c * 8 + sp) * 2 + 1) * 2 + 2)


class Rot:

    def __init__(self, sems, nbanks, preset=None):
        self.S = sems
        self.last = [None] * nbanks
        if preset:
            for i, ev in preset.items():
                self.last[i] = ev

    def acquire(self, q, b, free_evt):
        ev = self.last[b]
        if ev is not None:
            q.wait_ge(self.S[ev[0]], ev[1])
        self.last[b] = free_evt


def build_program(stop="full"):
    import os
    V_NOTM = os.environ.get("MK_NOTM") == "1"
    V_NOIP = os.environ.get("MK_NOIP") == "1"
    SLABS = _slab_table(stop)
    SLAB_IDX = {k: i for i, k in enumerate(SLABS)}
    if stop in ("p1y", "p0y"):
        STOPY = stop
        stop = "p1" if stop == "p1y" else "p0"
    else:
        STOPY = None
    V_NOSTAT = os.environ.get("MK_NOSTAT") == "1"
    V_STATDBG = os.environ.get("MK_STATDBG") == "1"
    nc = bass.Bass("TRN2", target_bir_lowering=False, debug=False,
                   detect_race_conditions=False)
    U8 = mybir.dt.uint8

    xp = nc.dram_tensor("xp", [128, 8, D], F32, kind="ExternalInput").ap()
    w_in_p = nc.dram_tensor("w_in_p", [16, 128, 16, 256], F32, kind="ExternalInput").ap()
    w_out_p = nc.dram_tensor("w_out_p", [16, 128, 32, 128], F32, kind="ExternalInput").ap()
    wg_p = nc.dram_tensor("wg_p", [32, 128, 16, 256], U8, kind="ExternalInput").ap()
    wu_p = nc.dram_tensor("wu_p", [32, 128, 16, 256], U8, kind="ExternalInput").ap()
    wd_p = nc.dram_tensor("wd_p", [4, 8, 128, 16, 2, 128], U8, kind="ExternalInput").ap()
    bc_p = nc.dram_tensor("bc_p", [128, 4, 512], F32, kind="ExternalInput").ap()
    at_p = nc.dram_tensor("at_p", [128, 512], F32, kind="ExternalInput").ap()
    gs_p = nc.dram_tensor("gs_p", [128, 64], F32, kind="ExternalInput").ap()
    us_p = nc.dram_tensor("us_p", [128, 64], F32, kind="ExternalInput").ap()
    ds_p = nc.dram_tensor("ds_p", [128, 16], F32, kind="ExternalInput").ap()
    nw_p = nc.dram_tensor("nw_p", [128, 16], F32, kind="ExternalInput").ap()
    id_bf = nc.dram_tensor("id_bf", [128, 128], BF16, kind="ExternalInput").ap()
    id_f = nc.dram_tensor("id_f", [128, 128], F32, kind="ExternalInput").ap()
    outp = nc.dram_tensor("outp", [128, 16 * T], F32, kind="ExternalOutput").ap()

    ctx = ExitStack()
    e = ctx.enter_context

    accT = e(nc.sbuf_tensor("accT", [128, 16 * T], F32))
    xnT = e(nc.sbuf_tensor("xnT", [128, 16 * T], BF16))
    ringS = e(nc.sbuf_tensor("ringS", [128, RING * SLOT], BF16))
    identb = e(nc.sbuf_tensor("identb", [128, 128], BF16))
    identf = e(nc.sbuf_tensor("identf", [128, 128], F32))
    gs = e(nc.sbuf_tensor("gs", [128, 64], F32))
    us = e(nc.sbuf_tensor("us", [128, 64], F32))
    ds = e(nc.sbuf_tensor("ds", [128, 16], F32))
    nw = e(nc.sbuf_tensor("nw", [128, 16], F32))
    K = e(nc.sbuf_tensor("K", [128, 32], F32))
    c8 = e(nc.sbuf_tensor("c8", [128, 5], F32))
    ones_r = e(nc.sbuf_tensor("ones_r", [1, 128], BF16))
    ones_sq = e(nc.sbuf_tensor("ones_sq", [128, 128], BF16))
    ssrow = e(nc.sbuf_tensor("ssrow", [1, 512], F32))
    lnr = e(nc.sbuf_tensor("lnr", [1, 512], F32))
    rstd_r = e(nc.sbuf_tensor("rstd_r", [1, 512], F32))
    rstd_bfr = e(nc.sbuf_tensor("rstd_bfr", [1, 512], BF16))
    uscr = e(nc.sbuf_tensor("uscr", [128, 1024], F32))
    stat = e(nc.sbuf_tensor("stat", [128, 10], F32))
    silu = uscr
    tmpA = uscr[:, 0:512]

    ps03 = e(nc.psum_tensor("ps03", [128, 2048], F32))
    ps45 = e(nc.psum_tensor("ps45", [128, 1024], F32))
    ps6 = e(nc.psum_tensor("ps6", [128, 512], F32))
    ps7 = e(nc.psum_tensor("ps7", [128, 512], F32))

    def bank(i):
        return ps03[:, 512 * i:512 * (i + 1)]

    def dbank(i):
        return (ps45[:, 0:512], ps45[:, 512:1024], ps6[:, 0:512])[i]

    sem_names = ["sX0", "sX1", "sC", "sW0", "sW1", "sW2", "sW3", "sW4", "sWS",
                 "sTP", "sTPE", "sLN", "sAXN",
                 "sSQ0", "sVE0", "sVS", "sB7P", "sB7E", "sP1", "sE1", "sP2",
                 "sE2", "sSQH", "sVE2", "sAR2", "sRST", "sP2X",
                 "sAPP", "sP3g", "sP3u", "sSILU", "sH", "sD",
                 "sE3D", "sSt", "sCON", "sDV", "sB6E", "sAT"]
    S = {n: e(nc.semaphore(n)) for n in sem_names}

    S0 = 224 * 1024 - 32 - 56 * 1024
    assert nc.sbuf_base <= S0, (nc.sbuf_base, S0)
    at = nc.alloc_sbuf_tensor_at
    xsl = at("xsl", [128, 2 * D], F32, offset=S0 + 0).ap()
    xnb = at("xnb", [128, 2 * D], BF16, offset=S0 + 16384).ap()
    scr0 = at("scr0", [128, D], F32, offset=S0 + 24576).ap()
    bcs = at("bcs", [128, 2048], F32, offset=S0 + 32768).ap()
    scrK = at("scrK", [128, 1024], F32, offset=S0 + 40960).ap()
    ats = at("ats", [128, 512], F32, offset=S0 + 45056).ap()
    cst = at("cst", [128, 1024], F32, offset=S0 + 47104).ap()
    yT = at("yT", [128, 32 * 512], BF16, offset=S0 + 0).ap()
    ahT = at("ahT", [128, 16 * 512], BF16, offset=S0 + 32768).ap()
    sq2 = at("sq2", [128, 2 * 512], BF16, offset=S0 + 49152).ap()
    hT = at("hT", [128, 16 * T], BF16, offset=S0 + 0).ap()

    def slab_slot(idx):
        k = idx % RING
        return ringS[:, SLOT * k:SLOT * (k + 1)]

    def slab_dram(key):
        kind = key[0]
        if kind == "win":
            return w_in_p[key[2]]
        if kind == "wout":
            return w_out_p[key[2]]
        if kind == "g":
            return wg_p[key[1] * 8 + key[2]]
        if kind == "u":
            return wu_p[key[1] * 8 + key[2]]
        return wd_p[key[1], key[2]]

    blk = e(nc.Block())

    @blk.sync
    def _(q):
        q.dma_start(out=xsl[:, 0:D], in_=xp[:, 0, :]).then_inc(S["sX0"], 16)
        q.dma_start(out=identb[:], in_=id_bf[:, :]).then_inc(S["sC"], 16)
        q.dma_start(out=identf[:], in_=id_f[:, :]).then_inc(S["sC"], 16)
        q.dma_start(out=bcs[:], in_=bc_p.rearrange("p a b -> p (a b)")).then_inc(S["sC"], 16)
        q.dma_start(out=ats[:], in_=at_p[:, :]).then_inc(S["sC"], 16)
        q.dma_start(out=gs[:], in_=gs_p[:, :]).then_inc(S["sC"], 16)
        q.dma_start(out=us[:], in_=us_p[:, :]).then_inc(S["sC"], 16)
        q.dma_start(out=ds[:], in_=ds_p[:, :]).then_inc(S["sC"], 16)
        q.dma_start(out=nw[:], in_=nw_p[:, :]).then_inc(S["sC"], 16)
        for g in range(1, 8):
            if g >= 2:
                q.wait_ge(S["sTP"], 8 * (g - 1))
            q.dma_start(out=xsl[:, D * (g % 2):D * (g % 2 + 1)],
                        in_=xp[:, g, :]).then_inc(S["sX" + str(g % 2)], 16)
        if STOPY == "p1y":
            q.wait_ge(S["sE1"], 32)
            q.dma_start(out=outp[:, 0:8192],
                        in_=yT[:].bitcast(F32)).then_inc(S["sSt"], 16)
            q.wait_ge(S["sSt"], 16)
            return
        if STOPY == "p0y":
            q.wait_ge(S["sTPE"], 64)
            q.dma_start(out=outp[:, 0:8192],
                        in_=xnT[:].bitcast(F32)).then_inc(S["sSt"], 16)
            q.wait_ge(S["sSt"], 16)
            return
        if stop == "p0":
            q.wait_ge(S["sTPE"], 64)
        elif stop == "p1":
            q.wait_ge(S["sE1"], 64)
        elif stop == "p2a":
            q.wait_ge(S["sE2"], 32)
        elif stop == "attn":
            q.wait_ge(S["sAPP"], 2)
        else:
            for hoa in range(16):
                q.wait_ge(S["sE3D"], 98 + 2 * hoa)
                q.dma_start(out=outp[:, hoa * T:(hoa + 1) * T],
                            in_=accT[:, hoa * T:(hoa + 1) * T]).then_inc(S["sSt"], 16)
            q.wait_ge(S["sSt"], 256)
            return
        q.dma_start(out=outp[:, :], in_=accT[:]).then_inc(S["sSt"], 16)
        q.wait_ge(S["sSt"], 16)

    @blk.gpsimd
    def _(q):
        for n, key in enumerate(SLABS):
            if stop == "p0":
                break
            if stop == "attn" and n >= 64:
                break
            if n >= RING:
                ev = _slab_free_evt(SLABS[n - RING])
                q.wait_ge(S[ev[0]], ev[1])
            q.dma_start(out=slab_slot(n),
                        in_=slab_dram(key)).then_inc(S["sW" + str(n % RING)], 16)

    @blk.tensor
    def _(q):
        rot03 = Rot(S, 4)
        rotD = Rot(S, 3, preset={2: ("sB6E", 2)})
        rot7 = Rot(S, 1)
        b7grp = [0]
        sw_seen = [0] * RING
        sws_seen = [0]

        def need_slab(key):
            n = SLAB_IDX[key]
            if n >= N_ATTN_SLABS:
                t = n - N_ATTN_SLABS + 1
                if t > sws_seen[0]:
                    q.wait_ge(S["sWS"], t)
                    sws_seen[0] = t
                return
            t = 16 * (n // RING + 1)
            if t > sw_seen[n % RING]:
                q.wait_ge(S["sW" + str(n % RING)], t)
                sw_seen[n % RING] = t

        q.wait_ge(S["sC"], 128)

        for g in range(8):
            xs = xsl[:, D * (g % 2):D * (g % 2 + 1)]
            xb = xnb[:, D * (g % 2):D * (g % 2 + 1)]
            for b in range(4):
                jdx = 8 * g + b
                rot03.acquire(q, jdx % 4, ("sTPE", jdx + 1))
                if b == 0:
                    q.wait_ge(S["sLN"], g + 1)
                for mi in range(4):
                    hc = 4 * b + mi
                    last = q.matmul(bank(jdx % 4)[:, 128 * mi:128 * (mi + 1)],
                                    xb[:, 128 * hc:128 * (hc + 1)], identb[:],
                                    start=True, stop=True)
                last.then_inc(S["sTP"], 1)
            for b in range(4):
                jdx = 8 * g + 4 + b
                rot03.acquire(q, jdx % 4, ("sTPE", jdx + 1))
                if b == 0:
                    q.wait_ge(S["sAXN"], g + 1)
                axsrc = bcs if V_NOIP else xs
                for mi in range(4):
                    hc = 4 * b + mi
                    if V_NOTM:
                        last = q.matmul(bank(jdx % 4)[:, 128 * mi:128 * (mi + 1)],
                                        axsrc[:, 128 * hc:128 * (hc + 1)], identf[:],
                                        start=True, stop=True)
                    else:
                        last = q.transpose(bank(jdx % 4)[:, 128 * mi:128 * (mi + 1)],
                                           axsrc[:, 128 * hc:128 * (hc + 1)], identf[:])
                last.then_inc(S["sTP"], 1)

        if stop == "p0":
            return

        for h in range(2):
            q.wait_ge(S["sTPE"], 32 if h == 0 else 64)
            for s in range(16):
                key = ("win", h, s)
                need_slab(key)
                slab = slab_slot(SLAB_IDX[key])
                for jbl in range(2):
                    lcl = s * 2 + jbl
                    n1 = h * 32 + lcl
                    rot03.acquire(q, lcl % 4, ("sE1", n1 + 1))
                    for hc in range(16):
                        last = q.matmul(
                            bank(lcl % 4),
                            slab[:, hc * 256 + jbl * 128: hc * 256 + jbl * 128 + 128],
                            xnT[:, hc * T + h * 512: hc * T + h * 512 + 512],
                            start=(hc == 0), stop=(hc == 15))
                    last.then_inc(S["sP1"], 1)
            for hoc in range(17):
                if stop == "p1":
                    break
                if hoc < 16:
                    key = ("wout", h, hoc)
                    need_slab(key)
                    slab = slab_slot(SLAB_IDX[key])
                    k2 = 16 * h + hoc
                    if hoc == 0:
                        q.wait_ge(S["sE1"], 32 * (h + 1))
                    rot03.acquire(q, hoc % 4, ("sE2", k2 + 1))
                    for jc in range(32):
                        last = q.matmul(
                            bank(hoc % 4),
                            slab[:, jc * 128: jc * 128 + 128],
                            yT[:, jc * 512: jc * 512 + 512],
                            start=(jc == 0), stop=(jc == 31))
                    last.then_inc(S["sP2"], 1)
                if stop == "p2a":
                    continue
                if hoc >= 1:
                    k2s = 16 * h + hoc - 1
                    q.wait_ge(S["sSQH"], k2s + 1)
                    b7grp[0] += 1
                    rot7.acquire(q, 0, ("sB7E", b7grp[0]))
                    last = q.matmul(ps7[:, 0:512], ones_sq[:, 0:128],
                                    sq2[:, 512 * (k2s % 2): 512 * (k2s % 2) + 512],
                                    start=True, stop=True)
                    last.then_inc(S["sB7P"], 1)
            if stop in ("p1", "p2a"):
                continue
            q.wait_ge(S["sRST"], h + 1)
            if h == 1:
                q.wait_ge(S["sB6E"], 1)
            q.matmul(ps6[:, 0:512], ones_r[0:1, 0:128],
                     rstd_bfr[0:1, 0:512],
                     start=True, stop=True).then_inc(S["sP2X"], 1)

        if stop in ("attn", "p1", "p2a"):
            return
        Kp = 0
        dn = 0
        for c in range(4):
            for sp in range(8):
                gkey = ("g", c, sp)
                ukey = ("u", c, sp)
                need_slab(gkey)
                need_slab(ukey)
                gsl = slab_slot(SLAB_IDX[gkey])
                usl = slab_slot(SLAB_IDX[ukey])
                for ibl in range(2):
                    for th in range(2):
                        gb = 2 * (Kp % 2)
                        rot03.acquire(q, gb, ("sSILU", Kp + 1))
                        for hc in range(16):
                            last = q.matmul(
                                bank(gb),
                                gsl[:, hc * 256 + ibl * 128: hc * 256 + ibl * 128 + 128],
                                xnT[:, hc * T + th * 512: hc * T + th * 512 + 512],
                                start=(hc == 0), stop=(hc == 15))
                        last.then_inc(S["sP3g"], 1)
                        ub = 2 * (Kp % 2) + 1
                        rot03.acquire(q, ub, ("sH", Kp + 1))
                        for hc in range(16):
                            last = q.matmul(
                                bank(ub),
                                usl[:, hc * 256 + ibl * 128: hc * 256 + ibl * 128 + 128],
                                xnT[:, hc * T + th * 512: hc * T + th * 512 + 512],
                                start=(hc == 0), stop=(hc == 15))
                        last.then_inc(S["sP3u"], 1)
                        Kp += 1
            q.wait_ge(S["sH"], 32 * (c + 1))
            for sp in range(8):
                dkey = ("down", c, sp)
                need_slab(dkey)
                dsl = slab_slot(SLAB_IDX[dkey])
                for hob in range(2):
                    for th in range(2):
                        rotD.acquire(q, dn % 3, ("sE3D", dn + 1))
                        for it in range(16):
                            last = q.matmul(
                                dbank(dn % 3),
                                dsl[:, (it * 2 + hob) * 128: (it * 2 + hob) * 128 + 128],
                                hT[:, it * T + th * 512: it * T + th * 512 + 512],
                                start=(it == 0), stop=(it == 15))
                        last.then_inc(S["sD"], 1)
                        dn += 1

    @blk.vector
    def _(q):
        ts = q.tensor_scalar
        stt = q.scalar_tensor_tensor
        dvn = [0]

        q.memset(ones_r[:], 1.0)
        q.memset(ones_sq[:], 1.0)
        q.memset(c8[:, 0:1], -0.5)
        q.memset(c8[:, 4:5], -128.0)
        q.memset(c8[:, 1:2], 1.5707963267948966).then_inc(S["sCON"], 1)
        def fence0(inst):
            dvn[0] += 1
            inst.then_inc(S["sDV"], 1)
            q.wait_ge(S["sDV"], dvn[0])
        q.memset(c8[:, 3:4], 3.0)
        fence0(q.reciprocal(c8[:, 2:3], c8[:, 3:4]))
        fence0(ts(c8[:, 3:4], c8[:, 2:3], 3.0, 2.0, ALU.mult, ALU.subtract))
        fence0(ts(c8[:, 2:3], c8[:, 3:4], c8[:, 2:3], -1.0, ALU.mult, ALU.mult))
        fence0(ts(c8[:, 2:3], c8[:, 2:3], 4.0, 1.0, ALU.mult, ALU.subtract))
        fence0(ts(c8[:, 2:3], c8[:, 2:3], 3.0, 1.0, ALU.mult, ALU.subtract))
        fence0(ts(c8[:, 3:4], c8[:, 2:3], -1.0, None, ALU.mult))
        fence0(q.tensor_max(c8[:, 2:3], c8[:, 2:3], c8[:, 3:4]))
        q.wait_ge(S["sC"], 128)
        q.wait_ge(S["sAT"], 1)
        q.memset(scrK[:, 512:1024], 0.0)
        q.tensor_mul(uscr[:, 0:512], cst[:, 0:512], scrK[:, 512:1024])
        q.tensor_mul(uscr[:, 512:1024], cst[:, 512:1024], scrK[:, 512:1024])
        q.tensor_sub(uscr[:, 0:512], uscr[:, 0:512], uscr[:, 512:1024])
        q.tensor_add(scrK[:, 0:512], uscr[:, 0:512], bcs[:, 0:512])
        q.tensor_add(uscr[:, 512:1024], uscr[:, 512:1024], uscr[:, 0:512])
        q.tensor_add(uscr[:, 512:1024], uscr[:, 512:1024], bcs[:, 512:1024])
        q.tensor_mul(scrK[:, 0:512], scrK[:, 0:512], bcs[:, 1024:1536])
        q.tensor_mul(uscr[:, 512:1024], uscr[:, 512:1024], bcs[:, 1536:2048])
        q.tensor_sub(scrK[:, 0:512], scrK[:, 0:512], uscr[:, 512:1024])
        q.reduce_sum(K[:].rearrange("p (a b) -> p a b", b=1),
                     scrK[:, 0:512].rearrange("p (a b) -> p a b", a=32), AX)

        for g in range(8):
            xs = xsl[:, D * (g % 2):D * (g % 2 + 1)]
            xb = xnb[:, D * (g % 2):D * (g % 2 + 1)]
            if g >= 2:
                q.wait_ge(S["sTP"], 8 * (g - 1))
            q.wait_ge(S["sX" + str(g % 2)], 16 * (g // 2 + 1))
            if V_NOSTAT:
                q.memset(stat[:, 6:7], 1.0)
                q.memset(stat[:, 8:9], 0.0)
                q.memset(stat[:, 9:10], 2.0)
                ts(xb, xs, stat[:, 6:7], stat[:, 8:9],
                   ALU.mult, ALU.subtract).then_inc(S["sLN"], 1)
                ts(bcs if V_NOIP else xs, xs, stat[:, 9:10], stat[:, 8:9],
                   ALU.mult, ALU.subtract).then_inc(S["sAXN"], 1)
                for b in range(8):
                    jdx = 8 * g + b
                    q.wait_ge(S["sTP"], jdx + 1)
                    src = bank(jdx % 4).rearrange("p (a t) -> p a t", a=4)
                    if b < 4:
                        dst = xnT[:].rearrange("p (a t) -> p a t", a=16)[
                            :, 4 * b:4 * b + 4, g * 128:(g + 1) * 128]
                    else:
                        dst = accT[:].rearrange("p (a t) -> p a t", a=16)[
                            :, 4 * (b - 4):4 * (b - 4) + 4, g * 128:(g + 1) * 128]
                    q.tensor_copy(dst, src).then_inc(S["sTPE"], 1)
                continue
            def fence(inst):
                dvn[0] += 1
                inst.then_inc(S["sDV"], 1)
                q.wait_ge(S["sDV"], dvn[0])
            fence(q.reduce_sum(stat[:, 0:1], xs, AX))
            q.wait_ge(S["sSQ0"], g + 1)
            ts(stat[:, 2:3], stat[:, 0:1], 1.0 / D, None, ALU.mult)
            fence(ts(stat[:, 4:5], stat[:, 1:2], 1.0 / D, None, ALU.mult))
            fence(ts(stat[:, 3:4], stat[:, 2:3], stat[:, 2:3], None, ALU.mult))
            fence(ts(stat[:, 5:6], stat[:, 4:5], stat[:, 3:4], EPS_LN,
                     ALU.subtract, ALU.add))
            q.engine_nop().then_inc(S["sVE0"], 1)
            q.wait_ge(S["sVS"], g + 1)
            fence(ts(stat[:, 7:8], stat[:, 6:7], stat[:, 6:7], stat[:, 5:6],
                     ALU.mult, ALU.mult))
            fence(ts(stat[:, 7:8], stat[:, 7:8], -0.5, 1.5, ALU.mult, ALU.add))
            fence(ts(stat[:, 6:7], stat[:, 7:8], stat[:, 6:7], None, ALU.mult))
            ts(stat[:, 8:9], stat[:, 2:3], stat[:, 6:7], None, ALU.mult)
            fence(ts(stat[:, 9:10], stat[:, 6:7], 1.0, None, ALU.add))
            ts(xb, xs, stat[:, 6:7], stat[:, 8:9],
               ALU.mult, ALU.subtract).then_inc(S["sLN"], 1)
            ts(bcs if V_NOIP else xs, xs, stat[:, 9:10], stat[:, 8:9],
               ALU.mult, ALU.subtract).then_inc(S["sAXN"], 1)
            for b in range(8):
                jdx = 8 * g + b
                q.wait_ge(S["sTP"], jdx + 1)
                src = bank(jdx % 4).rearrange("p (a t) -> p a t", a=4)
                if b < 4:
                    dst = xnT[:].rearrange("p (a t) -> p a t", a=16)[
                        :, 4 * b:4 * b + 4, g * 128:(g + 1) * 128]
                else:
                    dst = accT[:].rearrange("p (a t) -> p a t", a=16)[
                        :, 4 * (b - 4):4 * (b - 4) + 4, g * 128:(g + 1) * 128]
                q.tensor_copy(dst, src).then_inc(S["sTPE"], 1)

        if stop == "p0":
            return
        for h in range(2):
            for lcl in range(32):
                n1 = h * 32 + lcl
                jc = lcl
                q.wait_ge(S["sP1"], n1 + 1)
                ts(yT[:, jc * 512: jc * 512 + 512], bank(lcl % 4),
                   K[:, jc:jc + 1], None, ALU.mult).then_inc(S["sE1"], 1)
            if stop == "p1":
                continue
            for hoc in range(17):
                if hoc < 16:
                    k2 = 16 * h + hoc
                    q.wait_ge(S["sP2"], k2 + 1)
                    q.tensor_copy(ahT[:, hoc * 512: hoc * 512 + 512],
                                  bank(hoc % 4)).then_inc(S["sE2"], 1)
                if stop == "p2a":
                    continue
                if hoc >= 1:
                    hs = hoc - 1
                    q.wait_ge(S["sB7P"], h * 16 + hs + 1)
                    if hs == 0:
                        q.tensor_copy(ssrow[0:1, 0:512],
                                      ps7[0:1, 0:512]).then_inc(S["sB7E"], 1)
                    else:
                        q.tensor_add(ssrow[0:1, 0:512], ssrow[0:1, 0:512],
                                     ps7[0:1, 0:512]).then_inc(S["sB7E"], 1)
            if stop == "p2a":
                continue
            sr = ssrow[0:1, 0:512]
            ts(sr, sr, 1.0 / D, c8[0:1, 2:3], ALU.mult, ALU.add).then_inc(S["sVE2"], 1)
            q.wait_ge(S["sAR2"], h + 1)
            rr = rstd_r[0:1, 0:512]
            lr = lnr[0:1, 0:512]
            q.tensor_mul(lr, rr, rr)
            q.tensor_mul(lr, lr, sr)
            ts(lr, lr, -0.5, 1.5, ALU.mult, ALU.add)
            q.tensor_mul(rr, rr, lr)
            q.tensor_copy(rstd_bfr[0:1, 0:512], rr).then_inc(S["sRST"], 1)
            q.wait_ge(S["sP2X"], h + 1)
            q.tensor_copy(uscr[:, 512:1024], ps6[:, 0:512]).then_inc(S["sB6E"], 1)
            last = None
            for hoc in range(16):
                stt(tmpA, ahT[:, hoc * 512: hoc * 512 + 512],
                    nw[:, hoc:hoc + 1], uscr[:, 512:1024], ALU.mult, ALU.mult)
                a = accT[:, hoc * T + h * 512: hoc * T + h * 512 + 512]
                last = q.tensor_add(a, a, tmpA)
            last.then_inc(S["sAPP"], 1)

        if stop in ("attn", "p1", "p2a"):
            return
        Kp = 0
        dn = 0
        for c in range(4):
            if c >= 1:
                q.wait_ge(S["sD"], 32 * c)
            for sp in range(8):
                for ibl in range(2):
                    for th in range(2):
                        ib = c * 16 + sp * 2 + ibl
                        q.wait_ge(S["sP3u"], Kp + 1)
                        q.wait_ge(S["sSILU"], Kp + 1)
                        stt(hT[:, (sp * 2 + ibl) * T + th * 512:
                               (sp * 2 + ibl) * T + th * 512 + 512],
                            bank(2 * (Kp % 2) + 1), us[:, ib:ib + 1],
                            silu[:, 512 * (Kp % 2): 512 * (Kp % 2) + 512],
                            ALU.mult, ALU.mult).then_inc(S["sH"], 1)
                        Kp += 1
            for sp in range(8):
                for hob in range(2):
                    for th in range(2):
                        hoa = sp * 2 + hob
                        q.wait_ge(S["sD"], dn + 1)
                        a = accT[:, hoa * T + th * 512: hoa * T + th * 512 + 512]
                        stt(a, dbank(dn % 3), ds[:, hoa:hoa + 1], a,
                            ALU.mult, ALU.add).then_inc(S["sE3D"], 1)
                        dn += 1

    @blk.scalar
    def _(q):
        q.wait_ge(S["sC"], 128)
        q.wait_ge(S["sCON"], 1)
        q.activation(out=cst[:, 0:512], in_=ats[:], func=AF.Sin,
                     bias=c8[:, 1:2])
        q.activation(out=cst[:, 512:1024], in_=ats[:],
                     func=AF.Sin).then_inc(S["sAT"], 1)
        for g in range(8):
            if V_NOSTAT:
                break
            xs = xsl[:, D * (g % 2):D * (g % 2 + 1)]
            q.wait_ge(S["sX" + str(g % 2)], 16 * (g // 2 + 1))
            if g >= 1:
                q.wait_ge(S["sVE0"], g)
            q.activation(out=scr0[:], in_=xs, func=AF.Square,
                         accum_out=stat[:, 1:2]).then_inc(S["sSQ0"], 1)
            q.wait_ge(S["sVE0"], g + 1)
            q.activation(out=stat[:, 7:8], in_=stat[:, 5:6], func=AF.Ln)
            q.activation(out=stat[:, 6:7], in_=stat[:, 7:8], func=AF.Exp,
                         scale=c8[:, 0:1]).then_inc(S["sVS"], 1)
        if stop in ("p0", "p1", "p2a"):
            return
        ssw_seen = [0] * RING
        sub_q = [k for k in SLABS if k[0] in ("g", "u", "down")]
        nsub = [0]

        def sub_slab(key):
            n = SLAB_IDX[key]
            t = 16 * (n // RING + 1)
            if t > ssw_seen[n % RING]:
                q.wait_ge(S["sW" + str(n % RING)], t)
                ssw_seen[n % RING] = t
            sl = slab_slot(n)
            q.add(sl, sl, c8[:, 4:5]).then_inc(S["sWS"], 1)

        def pump(upto):
            while nsub[0] < min(upto, len(sub_q)):
                sub_slab(sub_q[nsub[0]])
                nsub[0] += 1

        for h in range(2):
            for hoc in range(16):
                k2 = 16 * h + hoc
                q.wait_ge(S["sE2"], k2 + 1)
                if k2 >= 2:
                    q.wait_ge(S["sB7P"], k2 - 1)
                q.activation(out=sq2[:, 512 * (k2 % 2): 512 * (k2 % 2) + 512],
                             in_=ahT[:, hoc * 512: hoc * 512 + 512],
                             func=AF.Square).then_inc(S["sSQH"], 1)
            if h == 1 and stop == "full":
                pump(2)
            q.wait_ge(S["sVE2"], h + 1)
            sr = ssrow[0:1, 0:512]
            lr = lnr[0:1, 0:512]
            rr = rstd_r[0:1, 0:512]
            q.activation(out=lr, in_=sr, func=AF.Ln)
            q.activation(out=rr, in_=lr, func=AF.Exp,
                         scale=c8[0:1, 0:1]).then_inc(S["sAR2"], 1)
        if stop in ("attn", "p1", "p2a"):
            return
        Kp = 0
        for c in range(4):
            base = c * 24
            for sp in range(8):
                pump(min(base + 2 * sp + 4, base + 18))
                for ibl in range(2):
                    for th in range(2):
                        ib = c * 16 + sp * 2 + ibl
                        q.wait_ge(S["sP3g"], Kp + 1)
                        if Kp == 0:
                            q.wait_ge(S["sAPP"], 2)
                        if Kp >= 2:
                            q.wait_ge(S["sH"], Kp - 1)
                        q.activation(out=silu[:, 512 * (Kp % 2): 512 * (Kp % 2) + 512],
                                     in_=bank(2 * (Kp % 2)), func=AF.Silu,
                                     scale=gs[:, ib:ib + 1]).then_inc(S["sSILU"], 1)
                        Kp += 1
            pump(base + 24)

    ctx.close()
    return nc


_CACHE = {}


def _get_nc():
    import os
    stop = os.environ.get("MK_STOP", "full")
    if stop not in _CACHE:
        _CACHE[stop] = build_program(stop)
    return _CACHE[stop]


def kernel(L0_attn_x, L0_attn_W_in, L0_attn_W_out, L0_attn_A_theta,
           L0_attn_B_real, L0_attn_B_imag, L0_attn_C_real, L0_attn_C_imag,
           L0_attn_norm_w, L0_ffn_gate_w, L0_ffn_gate_scale, L0_ffn_up_w,
           L0_ffn_up_scale, L0_ffn_down_w, L0_ffn_down_scale):
    bf = ml_dtypes.bfloat16
    f32 = np.float32
    i32 = np.int32

    x = np.ascontiguousarray(np.asarray(L0_attn_x, dtype=f32).reshape(8192, D))
    W_in = np.asarray(L0_attn_W_in, dtype=f32)
    W_out = np.asarray(L0_attn_W_out, dtype=f32)
    gate_w = np.asarray(L0_ffn_gate_w, dtype=i32).astype(np.uint8)
    up_w = np.asarray(L0_ffn_up_w, dtype=i32).astype(np.uint8)
    down_w = np.asarray(L0_ffn_down_w, dtype=i32).astype(np.uint8)

    w_in_p = np.ascontiguousarray(W_in.reshape(16, 256, 16, 128).transpose(0, 3, 2, 1))
    w_out_p = np.ascontiguousarray(W_out.reshape(16, 128, 32, 128).transpose(0, 3, 2, 1))
    wg_p = np.ascontiguousarray(gate_w.reshape(32, 256, 16, 128).transpose(0, 3, 2, 1))
    wu_p = np.ascontiguousarray(up_w.reshape(32, 256, 16, 128).transpose(0, 3, 2, 1))
    wd_p = np.ascontiguousarray(
        down_w.T.reshape(4, 16, 128, 8, 2, 128).transpose(0, 3, 2, 1, 4, 5))

    def row32(v):
        return np.ascontiguousarray(np.asarray(v, dtype=f32).reshape(32, 128).T)

    bc = np.stack([
        np.asarray(L0_attn_B_real, f32).reshape(4096, 16),
        np.asarray(L0_attn_B_imag, f32).reshape(4096, 16),
        np.asarray(L0_attn_C_real, f32).reshape(4096, 16),
        np.asarray(L0_attn_C_imag, f32).reshape(4096, 16),
    ])
    bc_p = np.ascontiguousarray(
        bc.reshape(4, 32, 128, 16).transpose(2, 0, 1, 3).reshape(128, 4, 512))
    at_p = np.ascontiguousarray(
        np.asarray(L0_attn_A_theta, f32).reshape(4096, 16)
        .reshape(32, 128, 16).transpose(1, 0, 2).reshape(128, 512))
    gs_p = np.ascontiguousarray(np.asarray(L0_ffn_gate_scale, f32).reshape(64, 128).T)
    us_p = np.ascontiguousarray(np.asarray(L0_ffn_up_scale, f32).reshape(64, 128).T)
    ds_p = np.ascontiguousarray(np.asarray(L0_ffn_down_scale, f32).reshape(16, 128).T)
    nw_p = np.ascontiguousarray(np.asarray(L0_attn_norm_w, f32).reshape(16, 128).T)
    id_bf = np.eye(128, dtype=bf)
    id_f = np.eye(128, dtype=f32)

    shared = dict(w_in_p=w_in_p, w_out_p=w_out_p, wg_p=wg_p, wu_p=wu_p,
                  wd_p=wd_p, bc_p=bc_p, at_p=at_p, gs_p=gs_p, us_p=us_p, ds_p=ds_p,
                  nw_p=nw_p, id_bf=id_bf, id_f=id_f)
    in_maps = []
    for c in range(8):
        xs = x[c * T:(c + 1) * T].reshape(8, 128, D).transpose(1, 0, 2)
        m = dict(shared)
        m["xp"] = np.ascontiguousarray(xs)
        in_maps.append(m)

    nc = _get_nc()
    res = run_bass_kernel_spmd(nc, in_maps, core_ids=list(range(8)))

    out = np.empty((8192, D), dtype=f32)
    for c in range(8):
        o = np.asarray(res.results[c]["outp"]).reshape(128, 16, T)
        out[c * T:(c + 1) * T] = o.transpose(2, 1, 0).reshape(T, D)
    return out.reshape(1, 8192, D)
